# version 08 (final)
# baseline (speedup 1.0000x reference)
import numpy as np
import concourse.bass as bass
import concourse.mybir as mybir
from concourse.bass_utils import run_bass_kernel_spmd

F32 = mybir.dt.float32
BF16 = mybir.dt.bfloat16
I32 = mybir.dt.int32
AF = mybir.ActivationFunctionType
ALU = mybir.AluOpType
AX = mybir.AxisListType.X

NCORES = 8
B, T, E, H, S, DH = 2, 2048, 1024, 16, 64, 4096
TT, QB, EC, HC = T // 128, T // 512, E // 128, DH // 128
EPS = 1e-5
NO_NORMS = False
NORM_LEVEL = 9
SQ_MODE = 0
DBG_SCRATCH = False
NO_PROJSTORE = False


class Op:
    __slots__ = ("eng", "fn", "deps", "sig", "sigval")

    def __init__(self, eng, fn, deps):
        self.eng, self.fn, self.deps = eng, fn, deps
        self.sig, self.sigval = False, 0


class Asm:
    ENGS = ("d", "g", "v", "s", "t")

    def __init__(self):
        self.ops = {e: [] for e in self.ENGS}
        self.fence = {}

    def op(self, eng, fn, deps=()):
        flat = []
        for d in deps:
            if d is None:
                continue
            if isinstance(d, (list, tuple)):
                flat.extend(x for x in d if x is not None)
            else:
                flat.append(d)
        if eng in ("d", "g") and self.fence.get(eng) is not None:
            flat.append(self.fence[eng])
        o = Op(eng, fn, flat)
        for dp in flat:
            if dp.eng != eng or eng in ("d", "g", "v", "s"):
                dp.sig = True
        self.ops[eng].append(o)
        return o

    def barrier(self):
        tails = {e: self.ops[e][-1] for e in self.ENGS if self.ops[e]}
        self.fence = {}
        nops = {}
        for e in list(tails):
            deps = [t for e2, t in tails.items() if e2 != e] + [tails[e]]
            nops[e] = self.op(e, lambda eng: eng.nop(), deps)
        for e in ("d", "g"):
            if e in nops:
                self.fence[e] = nops[e]

    def emit(self, nc, block, es):
        sems = {e: es.enter_context(nc.semaphore("sem_" + e))
                for e in self.ENGS if self.ops[e]}
        for e, lst in self.ops.items():
            step = 16 if e == "d" else 1
            c = 0
            for o in lst:
                if e == "d":
                    o.sig = True
                if o.sig:
                    c += step
                    o.sigval = c
        asm = self

        def body_for(e):
            def run(eng):
                last_wait = {}
                for o in asm.ops[e]:
                    for dp in o.deps:
                        if dp.eng == e and e == "t" and not dp.sig:
                            continue
                        v = dp.sigval
                        if v <= 0:
                            raise RuntimeError("dep without signal")
                        if last_wait.get(dp.eng, -1) >= v:
                            continue
                        eng.wait_ge(sems[dp.eng], v)
                        last_wait[dp.eng] = v
                    instr = o.fn(eng)
                    if o.sig:
                        if instr is None:
                            instr = eng.nop()
                        instr.then_inc(sems[e], 16 if e == "d" else 1)
            return run

        for e, deco in (("d", block.sync), ("g", block.gpsimd), ("v", block.vector),
                        ("s", block.scalar), ("t", block.tensor)):
            if self.ops[e]:
                deco(body_for(e))
        return sems


def build(debug_stage=0):
    nc = bass.Bass("TRN2", target_bir_lowering=False, detect_race_conditions=False)
    x_in = nc.dram_tensor("x", [B, T, E], F32, kind="ExternalInput").ap()
    W_in = {w: nc.dram_tensor(w, [E, E], F32, kind="ExternalInput").ap()
            for w in ("Wk", "Wq", "Wv", "Wu")}
    bu_in = nc.dram_tensor("bu", [E], F32, kind="ExternalInput").ap()
    w1_in = nc.dram_tensor("w1", [DH, E], F32, kind="ExternalInput").ap()
    cw_in = nc.dram_tensor("cw", [DH, 3], F32, kind="ExternalInput").ap()
    cb_in = nc.dram_tensor("cb", [DH], F32, kind="ExternalInput").ap()
    w2_in = nc.dram_tensor("w2", [E, DH], F32, kind="ExternalInput").ap()
    y_out = nc.dram_tensor("y", [B, T, E], F32, kind="ExternalOutput").ap()

    KI = "ExternalOutput" if DBG_SCRATCH else "Internal"
    wTd = {w + hl: nc.dram_tensor(f"wT_{w}_{hl}", [EC, 128, E], BF16, kind=KI).ap()
           for w in ("Wk", "Wq", "Wv", "Wu") for hl in ("hi", "lo")}
    w1Td = nc.dram_tensor("w1T", [EC, 128, DH], BF16, kind=KI).ap()
    w2Td = nc.dram_tensor("w2T", [HC, 128, E], BF16, kind=KI).ap()
    qkd = {nm: nc.dram_tensor("sc_" + nm, [EC, 128, T], BF16, kind=KI).ap()
           for nm in ("qhi", "qlo", "khi", "klo")}
    vd = {nm: nc.dram_tensor("sc_" + nm, [H, 128, TT * 65], BF16, kind=KI).ap()
          for nm in ("vhi", "vlo")}
    avd = {nm: nc.dram_tensor("sc_" + nm, [EC, 128, T], BF16, kind=KI).ap()
           for nm in ("avhi", "avlo")}
    xpd = nc.dram_tensor("xp_sc", [T, E], F32, kind=KI).ap()
    sld = nc.dram_tensor("sl_sc", [HC, 128, T], F32, kind=KI).ap()

    dbg = {}

    def DBG(name, shape, dt=F32):
        ap = nc.dram_tensor("dbg_" + name, list(shape), dt, kind="ExternalOutput").ap()
        dbg["dbg_" + name] = ap
        return ap

    from contextlib import ExitStack
    es = ExitStack()
    with es:
        def SBt(name, p, f, dt=F32):
            return es.enter_context(nc.sbuf_tensor(name, [p, f], dt))

        ident = SBt("ident", 128, 128)
        onesf = SBt("onesf", 128, 512)
        onesb = SBt("onesb", 128, 128, BF16)
        obcol = SBt("obcol", 128, 2, BF16)
        burow = SBt("burow", 1, E)
        cwc = SBt("cwc", 128, 3 * HC)
        cbc = SBt("cbc", 128, HC)
        swc = SBt("swc", 128, 2)
        iswc = SBt("iswc", 128, 2)
        Bneg = SBt("Bneg", 128, H)
        nqc = SBt("nqc", 128, H)
        nkc = SBt("nkc", 128, H)
        inv1c = SBt("inv1c", 128, TT)
        c2c = SBt("c2c", 128, TT)
        ctc = SBt("ctc", 128, TT)
        swcg = SBt("swcg", 128, 2)
        cwcg = SBt("cwcg", 128, 3 * HC)
        cbcg = SBt("cbcg", 128, HC)
        invg = SBt("invg", 128, TT)
        ctcg = SBt("ctcg", 128, TT)
        rw = SBt("rw", 128, 32)
        BFA = SBt("BFA", 128, 49152, BF16)
        FPA = SBt("FPA", 128, 13312)
        QIa = SBt("QIa", 128, 2048, I32)
        P32 = [SBt(f"P32_{i}", 128, 512) for i in range(3)]
        PHt = [SBt(f"PH_{i}", 128, 512, BF16) for i in range(3)]
        PLt = [SBt(f"PL_{i}", 128, 512, BF16) for i in range(3)]
        SQt = SBt("SQt", 128, 512, BF16)
        SGB = SBt("SGB", 128, 512)
        VST = {hl: SBt("VST_" + hl, 128, 2 * 520, BF16) for hl in ("hi", "lo")}
        PS = [es.enter_context(nc.psum_tensor(f"PS{i}", [128, 512], F32))
              for i in range(8)]

        block = es.enter_context(nc.Block())
        A = Asm()

        def dma(dst, src, deps=()):
            return A.op("d", lambda e, d_=dst, s_=src: e.dma_start(out=d_, in_=s_), deps)

        def dma_nc(dst, src, deps=()):
            return A.op("d", lambda e, d_=dst, s_=src: e.dma_start(
                out=d_, in_=s_, allow_slow_non_contiguous=True), deps)

        def gop(fn, deps=()):
            return A.op("g", fn, deps)

        def vop(fn, deps=()):
            return A.op("v", fn, deps)

        def sop(fn, deps=()):
            return A.op("s", fn, deps)

        def top(fn, deps=()):
            return A.op("t", fn, deps)

        def bfa(off, n):
            return BFA[:, off:off + n]

        def fpa(off, n):
            return FPA[:, off:off + n]

        def vchain(fns, deps=()):
            prev = list(deps)
            o = None
            for fn in fns:
                o = vop(fn, deps=prev)
                prev = [o]
            return o

        tr_state = {"rot": [None, None], "cnt": 0}

        def transpose_f32(dst_fn, src_fn, deps):
            k = tr_state["cnt"]
            tr_state["cnt"] += 1
            slot = 6 + (k % 2)
            pdep = tr_state["rot"][k % 2]
            t_ = top(lambda t, sl=slot, sf=src_fn: t.transpose(PS[sl][:, 0:128], sf(),
                                                               ident[:]),
                     deps=list(deps) + [pdep, c_id])
            e_ = vop(lambda v, sl=slot, df=dst_fn: v.tensor_copy(df(), PS[sl][:, 0:128]),
                     deps=[t_])
            tr_state["rot"][k % 2] = e_
            return e_

        c_id0 = gop(lambda g: g.memset(ident[:], 1.0))
        c_id = gop(lambda g: g.affine_select(out=ident[:], in_=ident[:],
                                             compare_op=ALU.is_equal, fill=0.0, base=0,
                                             pattern=[[-1, 128]], channel_multiplier=1),
                   deps=[c_id0])
        c_ones = gop(lambda g: g.memset(onesf[:], 1.0))
        c_onesb = gop(lambda g: g.memset(onesb[:], 1.0))
        c_obcol = gop(lambda g: (g.memset(obcol[:, 0:1], 1.0),
                                 g.memset(obcol[:, 1:2], 0.0))[-1])
        def _vst_init(g):
            ins = None
            for sl in range(2):
                for hh in range(8):
                    ins = g.memset(VST["hi"][:, sl * 520 + hh * 65 + 64:
                                             sl * 520 + hh * 65 + 65], 1.0)
                    ins = g.memset(VST["lo"][:, sl * 520 + hh * 65 + 64:
                                             sl * 520 + hh * 65 + 65], 0.0)
            return ins
        c_vst = gop(_vst_init)
        c_bu = dma(burow[0:1, :], bu_in[None, :])
        c_cw = dma_nc(cwc[:, :].rearrange("p (c k) -> p c k", k=3),
                      cw_in.rearrange("(c p) k -> p c k", p=128))
        c_cb = dma_nc(cbc[:, :], cb_in.rearrange("(c p) -> p c", p=128))
        g_cw = gop(lambda g: g.tensor_copy(cwcg[:, :], cwc[:, :]), deps=[c_cw])
        g_cb = gop(lambda g: g.tensor_copy(cbcg[:, :], cbc[:, :]), deps=[c_cb])

        def prep_attn_w(w, mask):
            src = W_in[w]

            def blocks(ic):
                return (range(ic, EC) if mask == "triu" else
                        range(0, ic + 1) if mask == "tril" else range(EC))

            slot_tail = [None, None]
            store_tail = {}
            for ic in range(EC):
                ls = ic % 2
                ld = dma(fpa(ls * 1024, 1024), src[ic * 128:(ic + 1) * 128, :],
                         deps=[slot_tail[ls]])
                mdep = ld
                if mask is not None:
                    sgn = 1 if mask == "triu" else -1
                    mdep = gop(lambda g, off=ls * 1024, ic=ic, sgn=sgn: g.affine_select(
                        out=fpa(off, 1024), in_=fpa(off, 1024), compare_op=ALU.is_ge,
                        fill=0.0, base=-sgn * 128 * ic, pattern=[[sgn, 1024]],
                        channel_multiplier=-sgn), deps=[ld, c_id])
                last = None
                for jc in blocks(ic):
                    bs = (ic * EC + jc) % 4
                    ev = transpose_f32(
                        lambda bs=bs: fpa(2048 + bs * 128, 128),
                        lambda off=ls * 1024, jc=jc: fpa(off + jc * 128, 128),
                        deps=[mdep])
                    sp = vop(lambda v, bs=bs: (
                        v.tensor_copy(bfa(bs * 128, 128), fpa(2048 + bs * 128, 128)),
                        v.tensor_sub(bfa(512 + bs * 128, 128), fpa(2048 + bs * 128, 128),
                                     bfa(bs * 128, 128)))[-1],
                        deps=[ev, store_tail.get(bs)])
                    dma(wTd[w + "hi"][jc, :, ic * 128:(ic + 1) * 128],
                        bfa(bs * 128, 128), deps=[sp])
                    s2 = dma(wTd[w + "lo"][jc, :, ic * 128:(ic + 1) * 128],
                             bfa(512 + bs * 128, 128), deps=[sp])
                    store_tail[bs] = s2
                    last = ev
                slot_tail[ls] = last

        prep_attn_w("Wk", "triu")
        prep_attn_w("Wq", "tril")
        prep_attn_w("Wv", None)
        prep_attn_w("Wu", None)
        A.barrier()

        def prep_ffn_w(which):
            w_src, nch, width, dstT = ((w1_in, HC, E, w1Td) if which == 1
                                       else (w2_in, EC, DH, w2Td))
            prev = None
            acc = None
            for c in range(nch):
                ld = dma(fpa(0, width), w_src[c * 128:(c + 1) * 128, :], deps=[prev])
                red = vchain([
                    lambda v, width=width: v.tensor_reduce(
                        rw[:, 1:2], fpa(0, width), axis=AX, op=ALU.add,
                        apply_absolute_value=True),
                    (lambda v: v.tensor_copy(rw[:, 0:1], rw[:, 1:2])) if c == 0 else
                    (lambda v: v.tensor_add(rw[:, 0:1], rw[:, 0:1], rw[:, 1:2])),
                ], deps=[ld, acc])
                acc = red
                prev = red
            mm = top(lambda t: t.matmul(PS[0][0:1, 0:1], rw[:, 0:1], onesf[:, 0:1],
                                        start=True, stop=True), deps=[acc, c_ones])
            rm = vchain([
                lambda v: v.tensor_scalar(rw[0:1, 2:3], PS[0][0:1, 0:1],
                                          1.0 / (DH * E), 1e-5, ALU.mult, ALU.max),
                lambda v: v.reciprocal(rw[0:1, 3:4], rw[0:1, 2:3]),
            ], deps=[mm])
            bc = top(lambda t: (
                t.matmul(PS[1][:, 0:1], onesf[0:1, 0:128], rw[0:1, 3:4],
                         start=True, stop=True),
                t.matmul(PS[1][:, 1:2], onesf[0:1, 0:128], rw[0:1, 2:3],
                         start=True, stop=True))[-1], deps=[rm, c_ones])
            cp = vop(lambda v, wh=which: (
                v.tensor_copy(swc[:, wh - 1:wh], PS[1][:, 0:1]),
                v.tensor_copy(iswc[:, wh - 1:wh], PS[1][:, 1:2]))[-1], deps=[bc])
            lsw = gop(lambda g, wh=which: g.tensor_copy(swcg[:, wh - 1:wh],
                                                        swc[:, wh - 1:wh]), deps=[cp])

            nblk = width // 128
            prev_c = cp
            store_tail = {}
            for c in range(nch):
                ld = dma(fpa(0, width), w_src[c * 128:(c + 1) * 128, :], deps=[prev_c])

                qc_ = None
                nhf = max(1, width // 2048)
                wp = width // nhf
                for hf in range(nhf):
                    lo = hf * wp
                    ca = vop(lambda v, lo=lo, wp=wp, wh=which: v.tensor_scalar(
                        fpa(4096, wp), fpa(lo, wp), swcg[:, wh - 1:wh], None, ALU.mult),
                        deps=[ld, qc_, lsw])
                    cb2 = vop(lambda v, wp=wp: v.tensor_copy(QIa[:, 0:wp], fpa(4096, wp)),
                              deps=[ca])
                    qc_ = vop(lambda v, lo=lo, wp=wp: (
                        v.tensor_tensor(fpa(6144, wp), fpa(4096, wp), QIa[:, 0:wp],
                                        ALU.subtract),
                        v.scalar_tensor_tensor(fpa(8192, wp), fpa(6144, wp), 0.5,
                                               QIa[:, 0:wp], ALU.is_gt, ALU.add),
                        v.scalar_tensor_tensor(fpa(8192, wp), fpa(6144, wp), -0.5,
                                               fpa(8192, wp), ALU.is_ge, ALU.add),
                        v.tensor_scalar(fpa(8192, wp), fpa(8192, wp), 1.0, -1.0,
                                        ALU.subtract, ALU.max),
                        v.tensor_scalar_min(fpa(lo, wp), fpa(8192, wp), 1.0))[-1],
                        deps=[cb2])
                last_ev = None
                for jc in range(nblk):
                    bs = jc % 4
                    ev = transpose_f32(lambda bs=bs: fpa(10240 + bs * 128, 128),
                                       lambda jc=jc: fpa(jc * 128, 128),
                                       deps=[qc_])
                    bf = vop(lambda v, bs=bs: v.tensor_copy(bfa(bs * 128, 128),
                                                            fpa(10240 + bs * 128, 128)),
                             deps=[ev, store_tail.get(bs)])
                    st = dma(dstT[jc, :, c * 128:(c + 1) * 128], bfa(bs * 128, 128),
                             deps=[bf])
                    store_tail[bs] = st
                    last_ev = ev
                prev_c = last_ev

        prep_ffn_w(1)
        prep_ffn_w(2)
        A.barrier()

        def stop_here():
            zz = vop(lambda v: v.memset(fpa(0, 1024), 0.0))
            outs = []
            for b_ in range(B):
                for tt_ in range(TT):
                    outs.append(dma(y_out[b_, tt_ * 128:(tt_ + 1) * 128, :],
                                    fpa(0, 1024), deps=[zz]))
            gop(lambda g: g.nop(), deps=[outs[-1]])
            A.emit(nc, block, es)

        if debug_stage == 1:
            stop_here()
            return nc, dbg

        L1H, L1L = 0, 16384
        WH, WL = 32768, 40960
        Q1T = 32768
        last_out = None
        for b in range(B):
            xslot_tail = [None, None]
            lnslot_tail = None
            prev_st1 = None
            for tt in range(TT):
                ls = tt % 2
                ld = dma(fpa(ls * 1024, 1024), x_in[b, tt * 128:(tt + 1) * 128, :],
                         deps=[xslot_tail[ls]])

                sqx = sop(lambda s, ls=ls: s.activation(
                    fpa(4096, 1024), fpa(ls * 1024, 1024), AF.Square),
                    deps=[ld, prev_st1])

                st1 = vchain([
                    lambda v, ls=ls: v.tensor_reduce(rw[:, 0:1], fpa(ls * 1024, 1024),
                                                     axis=AX, op=ALU.add),
                    lambda v: v.tensor_reduce(rw[:, 1:2], fpa(4096, 1024), axis=AX,
                                              op=ALU.add),
                    lambda v: v.tensor_scalar_mul(rw[:, 2:3], rw[:, 0:1], 1.0 / E),
                    lambda v: v.tensor_scalar_mul(rw[:, 3:4], rw[:, 1:2], 1.0 / E),
                    lambda v: v.scalar_tensor_tensor(rw[:, 4:5], rw[:, 2:3], 1.0,
                                                     rw[:, 2:3], ALU.mult, ALU.mult),
                    lambda v: v.tensor_tensor(rw[:, 5:6], rw[:, 3:4], rw[:, 4:5],
                                              ALU.subtract),
                    lambda v: v.tensor_scalar_add(rw[:, 5:6], rw[:, 5:6], EPS),
                ], deps=[ld, sqx])
                sq_ = sop(lambda s: s.activation(rw[:, 6:7], rw[:, 5:6], AF.Sqrt),
                          deps=[st1])
                vrec = vop(lambda v: v.reciprocal(rw[:, 7:8], rw[:, 6:7]), deps=[sq_])
                lau = gop(lambda g: (g.tensor_copy(rw[:, 24:25], rw[:, 2:3]),
                                     g.tensor_copy(rw[:, 25:26], rw[:, 7:8]))[-1],
                          deps=[vrec])
                ln = vop(lambda v, ls=ls: v.tensor_scalar(
                    fpa(2048, 1024), fpa(ls * 1024, 1024), rw[:, 24:25], rw[:, 25:26],
                    ALU.subtract, ALU.mult), deps=[lau, lnslot_tail])
                last = None
                for jc in range(EC):
                    ev = transpose_f32(lambda: fpa(3072, 128),
                                       lambda jc=jc: fpa(2048 + jc * 128, 128),
                                       deps=[ln])
                    sp = vop(lambda v, jc=jc, tt=tt: (
                        v.tensor_copy(bfa(L1H + jc * 2048 + tt * 128, 128), fpa(3072, 128)),
                        v.tensor_sub(bfa(L1L + jc * 2048 + tt * 128, 128), fpa(3072, 128),
                                     bfa(L1H + jc * 2048 + tt * 128, 128)))[-1],
                        deps=[ev])
                    tr_state["rot"][(tr_state["cnt"] - 1) % 2] = sp
                    last = sp
                lnslot_tail = last
                xslot_tail[ls] = [ln, sqx]
                prev_st1 = st1
            A.barrier()
            if DBG_SCRATCH and b == 0:
                d_h = DBG(f"l1h", (128, EC * 2048), BF16)
                d_l = DBG(f"l1l", (128, EC * 2048), BF16)
                dma(d_h, bfa(L1H, EC * 2048))
                dma(d_l, bfa(L1L, EC * 2048))
                dma(DBG("rwdump", (128, 32)), rw[:, :])
                dma(DBG("xslot", (128, 1024)), fpa(1024, 1024))
                dma(DBG("lnslot", (128, 1024)), fpa(2048, 1024))
                A.barrier()

            if debug_stage == 2:
                stop_here()
                return nc, dbg
            for wname, dhi, dlo, maskset, scaled, ncol in (
                    ("Wq", "qhi", "qlo", "tril", False, nqc),
                    ("Wk", "khi", "klo", "triu", True, nkc)):
                wld = []
                for jc in range(EC):
                    if maskset == "tril":
                        lo_i, hi_i = jc * 128, E
                    else:
                        lo_i, hi_i = 0, (jc + 1) * 128
                    wld.append(dma(bfa(WH + jc * 1024 + lo_i, hi_i - lo_i),
                                   wTd[wname + "hi"][jc, :, lo_i:hi_i]))
                    wld.append(dma(bfa(WL + jc * 1024 + lo_i, hi_i - lo_i),
                                   wTd[wname + "lo"][jc, :, lo_i:hi_i]))
                ps_dep = {0: None, 1: None, 2: None, 3: None}
                buf_dep = {0: None, 1: None, 2: None}
                nm_tail = None
                for ic in range(EC):
                    jcs = (list(range(0, ic + 1)) if maskset == "tril"
                           else list(range(ic, EC)))
                    for qc in range(QB):
                        k = ic * QB + qc
                        bank = k % 4
                        bs = k % 3
                        pb = 4 + (k % 2)

                        def mmq(t, ic=ic, qc=qc, jcs=jcs, bank=bank):
                            ins = None
                            for n_, jc in enumerate(jcs):
                                lh = bfa(WH + jc * 1024 + ic * 128, 128)
                                ll = bfa(WL + jc * 1024 + ic * 128, 128)
                                rh = bfa(L1H + jc * 2048 + qc * 512, 512)
                                rl = bfa(L1L + jc * 2048 + qc * 512, 512)
                                t.matmul(PS[bank][:], lh, rh, start=(n_ == 0), stop=False)
                                t.matmul(PS[bank][:], lh, rl, start=False, stop=False)
                                ins = t.matmul(PS[bank][:], ll, rh, start=False,
                                               stop=(n_ == len(jcs) - 1))
                            return ins
                        mm = top(mmq, deps=wld + [ps_dep[bank]])
                        sq_ = sop(lambda s, bank=bank: s.activation(
                            SQt[:], PS[bank][:], AF.Square), deps=[mm, nm_tail])
                        if scaled:
                            evh = vop(lambda v, bank=bank, bs=bs: v.tensor_scalar(
                                PHt[bs][:], PS[bank][:], 0.125, None, ALU.mult),
                                deps=[mm, buf_dep[bs], sq_])
                            evl = vop(lambda v, bank=bank, bs=bs: v.scalar_tensor_tensor(
                                PLt[bs][:], PS[bank][:], 0.125, PHt[bs][:],
                                ALU.mult, ALU.subtract), deps=[evh])
                        else:
                            evh = vop(lambda v, bank=bank, bs=bs: v.tensor_copy(
                                PHt[bs][:], PS[bank][:]), deps=[mm, buf_dep[bs], sq_])
                            evl = vop(lambda v, bank=bank, bs=bs: v.tensor_tensor(
                                PLt[bs][:], PS[bank][:], PHt[bs][:], ALU.subtract),
                                deps=[evh])
                        if NO_NORMS or NORM_LEVEL == 0:
                            s1_ = dma(qkd[dhi][ic, :, qc * 512:qc * 512 + 512], PHt[bs][:],
                                      deps=[evh])
                            s2_ = dma(qkd[dlo][ic, :, qc * 512:qc * 512 + 512], PLt[bs][:],
                                      deps=[evl])
                            ps_dep[bank] = [evl]
                            buf_dep[bs] = [s1_, s2_]
                            continue
                        if NORM_LEVEL == 1:
                            s1_ = dma(qkd[dhi][ic, :, qc * 512:qc * 512 + 512],
                                      PHt[bs][:], deps=[evh])
                            s2_ = dma(qkd[dlo][ic, :, qc * 512:qc * 512 + 512],
                                      PLt[bs][:], deps=[evl])
                            ps_dep[bank] = [evl, sq_]
                            buf_dep[bs] = [s1_, s2_]
                            nm_tail = sq_
                            continue
                        nm1 = top(lambda t, pb=pb: t.matmul(
                            PS[pb][:], onesb[0:64, :], SQt[0:64, :],
                            start=True, stop=True), deps=[sq_, c_onesb])
                        if NORM_LEVEL == 2:
                            red0 = vop(lambda v, pb=pb: v.tensor_reduce(
                                rw[:, 8:9], PS[pb][:], axis=AX, op=ALU.max), deps=[nm1])
                            s1_ = dma(qkd[dhi][ic, :, qc * 512:qc * 512 + 512],
                                      PHt[bs][:], deps=[evh])
                            s2_ = dma(qkd[dlo][ic, :, qc * 512:qc * 512 + 512],
                                      PLt[bs][:], deps=[evl])
                            ps_dep[bank] = [evl, sq_]
                            buf_dep[bs] = [s1_, s2_]
                            nm_tail = red0
                            continue
                        red1 = vchain([
                            lambda v, pb=pb: v.tensor_reduce(rw[:, 8:9], PS[pb][:],
                                                             axis=AX, op=ALU.max),
                            (lambda v, ic=ic, ncol=ncol:
                             v.tensor_copy(ncol[:, 2 * ic:2 * ic + 1], rw[:, 8:9]))
                            if qc == 0 else
                            (lambda v, ic=ic, ncol=ncol:
                             v.tensor_tensor(ncol[:, 2 * ic:2 * ic + 1],
                                             ncol[:, 2 * ic:2 * ic + 1], rw[:, 8:9],
                                             ALU.max)),
                        ], deps=[nm1])
                        nm2 = top(lambda t, pb=pb: t.matmul(
                            PS[pb][:], onesb[64:128, :], SQt[64:128, :],
                            start=True, stop=True), deps=[red1])
                        red2 = vchain([
                            lambda v, pb=pb: v.tensor_reduce(rw[:, 9:10], PS[pb][:],
                                                             axis=AX, op=ALU.max),
                            (lambda v, ic=ic, ncol=ncol:
                             v.tensor_copy(ncol[:, 2 * ic + 1:2 * ic + 2], rw[:, 9:10]))
                            if qc == 0 else
                            (lambda v, ic=ic, ncol=ncol:
                             v.tensor_tensor(ncol[:, 2 * ic + 1:2 * ic + 2],
                                             ncol[:, 2 * ic + 1:2 * ic + 2], rw[:, 9:10],
                                             ALU.max)),
                        ], deps=[nm2])
                        nm_tail = red2
                        s1_ = dma(qkd[dhi][ic, :, qc * 512:qc * 512 + 512], PHt[bs][:],
                                  deps=[evh])
                        s2_ = dma(qkd[dlo][ic, :, qc * 512:qc * 512 + 512], PLt[bs][:],
                                  deps=[evl])
                        ps_dep[bank] = [evl, sq_]
                        buf_dep[bs] = [s1_, s2_]
                A.barrier()
            if NO_NORMS:
                bb = vop(lambda v: v.memset(Bneg[:], -2.0))
            else:
                bb = vchain([
                    lambda v: v.tensor_tensor(Bneg[:], nqc[:], nkc[:], ALU.add),
                    lambda v: v.tensor_scalar(Bneg[:], Bneg[:], -0.125 * 0.505, -0.125,
                                              ALU.mult, ALU.add)])

            if debug_stage == 3:
                stop_here()
                return nc, dbg
            wld = []
            for jc in range(EC):
                wld.append(dma(bfa(WH + jc * 1024, 1024), wTd["Wvhi"][jc]))
                wld.append(dma(bfa(WL + jc * 1024, 1024), wTd["Wvlo"][jc]))
            ps_dep = {0: None, 1: None, 2: None, 3: None}
            buf_dep = {0: None, 1: None, 2: None}
            for tt in range(TT):
                for i2 in range(2):
                    k = tt * 2 + i2
                    bank = k % 4
                    bs = k % 3

                    def mmv(t, tt=tt, i2=i2, bank=bank):
                        ins = None
                        for jc in range(EC):
                            lh = bfa(L1H + jc * 2048 + tt * 128, 128)
                            llo = bfa(L1L + jc * 2048 + tt * 128, 128)
                            rh = bfa(WH + jc * 1024 + i2 * 512, 512)
                            rl = bfa(WL + jc * 1024 + i2 * 512, 512)
                            t.matmul(PS[bank][:], lh, rh, start=(jc == 0), stop=False)
                            t.matmul(PS[bank][:], lh, rl, start=False, stop=False)
                            ins = t.matmul(PS[bank][:], llo, rh, start=False,
                                           stop=(jc == EC - 1))
                        return ins
                    mm = top(mmv, deps=wld + [ps_dep[bank]])
                    vs = k % 2
                    evh = vop(lambda v, bank=bank, vs=vs: (
                        [v.tensor_copy(VST["hi"][:, vs * 520 + hh * 65:
                                                 vs * 520 + hh * 65 + 64],
                                       PS[bank][:, hh * 64:hh * 64 + 64])
                         for hh in range(8)])[-1],
                        deps=[mm, buf_dep[vs], c_vst])
                    evl = vop(lambda v, bank=bank, vs=vs: (
                        [v.scalar_tensor_tensor(
                            VST["lo"][:, vs * 520 + hh * 65:vs * 520 + hh * 65 + 64],
                            PS[bank][:, hh * 64:hh * 64 + 64], 1.0,
                            VST["hi"][:, vs * 520 + hh * 65:vs * 520 + hh * 65 + 64],
                            ALU.bypass, ALU.subtract)
                         for hh in range(8)])[-1], deps=[evh])
                    stores = []
                    for hh in range(8):
                        h = i2 * 8 + hh
                        stores.append(dma(vd["vhi"][h, :, tt * 65:tt * 65 + 65],
                                          VST["hi"][:, vs * 520 + hh * 65:
                                                     vs * 520 + hh * 65 + 65],
                                          deps=[evh]))
                        stores.append(dma(vd["vlo"][h, :, tt * 65:tt * 65 + 65],
                                          VST["lo"][:, vs * 520 + hh * 65:
                                                     vs * 520 + hh * 65 + 65],
                                          deps=[evl]))
                    ps_dep[bank] = [evl]
                    buf_dep[vs] = stores
            A.barrier()

            if debug_stage == 4:
                stop_here()
                return nc, dbg
            QKB, VB = 0, 16384
            pair_tail = [None, None]
            av_tail = None
            stage_tail = None
            prev_bcm = None
            prev_dv = None
            prev_sp = None
            for ic in range(EC):
                sl = ic % 2
                off = QKB + sl * 8192
                voff = VB + sl * 4160
                pt = pair_tail[sl]
                lds = [dma(bfa(off + 0, 2048), qkd["qhi"][ic], deps=[pt]),
                       dma(bfa(off + 2048, 2048), qkd["qlo"][ic], deps=[pt]),
                       dma(bfa(off + 4096, 2048), qkd["khi"][ic], deps=[pt]),
                       dma(bfa(off + 6144, 2048), qkd["klo"][ic], deps=[pt]),
                       dma(bfa(voff + 0, 1040), vd["vhi"][2 * ic], deps=[pt]),
                       dma(bfa(voff + 1040, 1040), vd["vlo"][2 * ic], deps=[pt]),
                       dma(bfa(voff + 2080, 1040), vd["vhi"][2 * ic + 1], deps=[pt]),
                       dma(bfa(voff + 3120, 1040), vd["vlo"][2 * ic + 1], deps=[pt])]
                last_avn = None
                for hh in range(2):
                    h = 2 * ic + hh
                    hb = hh * 64
                    vo = voff + hh * 2080
                    for qb in range(QB):
                        exp_dep = {0: None, 1: None}
                        split_dep = {0: None, 1: None, 2: None}
                        avs = []
                        for kt in range(TT):
                            sb = kt % 2
                            ps3 = kt % 3

                            def mms(t, off=off, hb=hb, kt=kt, qb=qb, sb=sb):
                                kh = BFA[:, off + 4096:off + 6144][hb:hb + 64,
                                                                   kt * 128:kt * 128 + 128]
                                kl = BFA[:, off + 6144:off + 8192][hb:hb + 64,
                                                                   kt * 128:kt * 128 + 128]
                                qh = BFA[:, off + 0:off + 2048][hb:hb + 64,
                                                                qb * 512:qb * 512 + 512]
                                ql = BFA[:, off + 2048:off + 4096][hb:hb + 64,
                                                                   qb * 512:qb * 512 + 512]
                                t.matmul(PS[sb][:], kh, qh, start=True, stop=False)
                                t.matmul(PS[sb][:], kh, ql, start=False, stop=False)
                                return t.matmul(PS[sb][:], kl, qh, start=False, stop=True)
                            mm = top(mms, deps=lds + [exp_dep[sb]])
                            ex = sop(lambda s, sb=sb, ps3=ps3, h=h: s.activation(
                                P32[ps3][:], PS[sb][:], AF.Exp, bias=Bneg[:, h:h + 1],
                                scale=1.0), deps=[mm, bb, split_dep[ps3]])
                            exp_dep[sb] = ex
                            hi = vop(lambda v, ps3=ps3: v.tensor_copy(PHt[ps3][:],
                                                                      P32[ps3][:]),
                                     deps=[ex])
                            lo = gop(lambda g, ps3=ps3: g.tensor_sub(PLt[ps3][:],
                                                                     P32[ps3][:],
                                                                     PHt[ps3][:]),
                                     deps=[hi])

                            def mav(t, vo=vo, kt=kt, ps3=ps3, first=(kt == 0),
                                    lastk=(kt == TT - 1)):
                                vhs = BFA[:, vo:vo + 1040][:, kt * 65:kt * 65 + 65]
                                vls = BFA[:, vo + 1040:vo + 2080][:, kt * 65:kt * 65 + 65]
                                t.matmul(PS[2][0:65, :], vhs, PHt[ps3][:],
                                         start=first, stop=False)
                                t.matmul(PS[2][0:65, :], vls, PHt[ps3][:],
                                         start=False, stop=False)
                                return t.matmul(PS[2][0:65, :], vhs, PLt[ps3][:],
                                                start=False, stop=lastk)
                            av = top(mav, deps=[lo] + ([av_tail] if kt == 0 else []))
                            split_dep[ps3] = av
                            avs.append(av)
                        sg1 = vchain([
                            lambda v: v.tensor_copy(SGB[64:65, :], PS[2][64:65, :]),
                            lambda v: v.reciprocal(SGB[64:65, :], SGB[64:65, :]),
                        ], deps=[avs[-1], prev_bcm])
                        bcm = top(lambda t: t.matmul(PS[3][0:64, :], onesf[64:65, 0:64],
                                                     SGB[64:65, :], start=True, stop=True),
                                  deps=[sg1, c_ones, prev_dv])
                        dv = vchain([
                            lambda v: v.tensor_copy(SGB[0:64, :], PS[3][0:64, :]),
                            lambda v: v.tensor_tensor(fpa(0, 512)[0:64, :],
                                                      PS[2][0:64, :], SGB[0:64, :],
                                                      ALU.mult),
                        ], deps=[bcm, prev_sp])
                        sp = vchain([
                            lambda v: v.tensor_copy(bfa(48128, 512)[0:64, :],
                                                    fpa(0, 512)[0:64, :]),
                            lambda v: v.tensor_sub(bfa(48640, 512)[0:64, :],
                                                   fpa(0, 512)[0:64, :],
                                                   bfa(48128, 512)[0:64, :]),
                        ], deps=[dv, stage_tail])
                        prev_bcm, prev_dv, prev_sp = bcm, dv, sp
                        st1 = dma(avd["avhi"][ic, hb:hb + 64, qb * 512:qb * 512 + 512],
                                  bfa(48128, 512)[0:64, :], deps=[sp])
                        st2 = dma(avd["avlo"][ic, hb:hb + 64, qb * 512:qb * 512 + 512],
                                  bfa(48640, 512)[0:64, :], deps=[sp])
                        stage_tail = [st1, st2]
                        av_tail = dv
                        last_avn = sp
                pair_tail[sl] = last_avn
            A.barrier()

            if debug_stage == 5:
                stop_here()
                return nc, dbg
            wld = []
            for jc in range(EC):
                wld.append(dma(bfa(0 + jc * 1024, 1024), wTd["Wuhi"][jc]))
                wld.append(dma(bfa(8192 + jc * 1024, 1024), wTd["Wulo"][jc]))
            q1_tail = None
            prev_mm = None
            prev_qn = None
            prev_ap = None
            prev_stx = None
            inv_lau = [None] * TT
            for qb in range(QB):
                avl = []
                for sc_ in range(EC):
                    avl.append(dma(bfa(16384 + sc_ * 512, 512),
                                   avd["avhi"][sc_][:, qb * 512:qb * 512 + 512],
                                   deps=[prev_mm, q1_tail]))
                    avl.append(dma(bfa(20480 + sc_ * 512, 512),
                                   avd["avlo"][sc_][:, qb * 512:qb * 512 + 512],
                                   deps=[prev_mm, q1_tail]))
                atm_done = []
                ps_dep = {0: None, 1: None}
                for ic in range(EC):
                    bank = ic % 2

                    def mmp(t, ic=ic, bank=bank):
                        t.matmul(PS[bank][:], burow[0:1, ic * 128:ic * 128 + 128],
                                 onesf[0:1, 0:512], start=True, stop=False)
                        ins = None
                        for sc_ in range(EC):
                            lh = bfa(0 + sc_ * 1024 + ic * 128, 128)
                            ll = bfa(8192 + sc_ * 1024 + ic * 128, 128)
                            rh = bfa(16384 + sc_ * 512, 512)
                            rl = bfa(20480 + sc_ * 512, 512)
                            t.matmul(PS[bank][:], lh, rh, start=False, stop=False)
                            t.matmul(PS[bank][:], lh, rl, start=False, stop=False)
                            ins = t.matmul(PS[bank][:], ll, rh, start=False,
                                           stop=(sc_ == EC - 1))
                        return ins
                    mm = top(mmp, deps=wld + avl + [c_bu, c_ones, ps_dep[bank]])
                    prev_mm = mm
                    ev = vop(lambda v, bank=bank: v.tensor_copy(
                        fpa(bank * 512, 512), PS[bank][:]), deps=[mm])
                    ps_dep[bank] = ev
                    for ts_ in range(4):
                        e2 = transpose_f32(
                            lambda ic=ic, ts_=ts_: FPA[:, 1024 + ts_ * 1024 + ic * 128:
                                                       1024 + ts_ * 1024 + ic * 128 + 128],
                            lambda bank=bank, ts_=ts_: fpa(bank * 512 + ts_ * 128, 128),
                            deps=[ev])
                        atm_done.append(e2)
                for ts_ in range(4):
                    tg = qb * 4 + ts_
                    xld = dma(fpa(5120, 1024), x_in[b, tg * 128:(tg + 1) * 128, :],
                              deps=[prev_ap])

                    sqD = sop(lambda s, ts_=ts_: s.activation(
                        fpa(7168, 1024), FPA[:, 1024 + ts_ * 1024:2048 + ts_ * 1024],
                        AF.Square), deps=atm_done + [prev_qn])

                    l1 = vchain([
                        lambda v, ts_=ts_: v.tensor_reduce(
                            rw[:, 0:1], FPA[:, 1024 + ts_ * 1024:2048 + ts_ * 1024],
                            axis=AX, op=ALU.add),
                        lambda v: v.tensor_reduce(rw[:, 1:2], fpa(7168, 1024), axis=AX,
                                                  op=ALU.add),
                        lambda v: v.tensor_scalar_mul(rw[:, 2:3], rw[:, 0:1], 1.0 / E),
                        lambda v: v.tensor_scalar_mul(rw[:, 3:4], rw[:, 1:2], 1.0 / E),
                        lambda v: v.scalar_tensor_tensor(rw[:, 4:5], rw[:, 2:3], 1.0,
                                                         rw[:, 2:3], ALU.mult, ALU.mult),
                        lambda v: v.tensor_tensor(rw[:, 5:6], rw[:, 3:4], rw[:, 4:5],
                                                  ALU.subtract),
                        lambda v: v.tensor_scalar_add(rw[:, 5:6], rw[:, 5:6], EPS),
                    ], deps=atm_done + [sqD])
                    sqo = sop(lambda s: s.activation(rw[:, 6:7], rw[:, 5:6], AF.Sqrt),
                              deps=[l1])
                    vr1 = vop(lambda v: v.reciprocal(rw[:, 7:8], rw[:, 6:7]),
                              deps=[sqo])
                    lau1 = gop(lambda g: (g.tensor_copy(rw[:, 24:25], rw[:, 2:3]),
                                          g.tensor_copy(rw[:, 25:26], rw[:, 7:8]))[-1],
                               deps=[vr1])
                    lnap = vop(lambda v, ts_=ts_: v.tensor_scalar(
                        fpa(7168, 1024), FPA[:, 1024 + ts_ * 1024:2048 + ts_ * 1024],
                        rw[:, 24:25], rw[:, 25:26], ALU.subtract, ALU.mult),
                        deps=[lau1, prev_qn, q1_tail, prev_stx])
                    ap_ = vop(lambda v: v.tensor_add(fpa(6144, 1024), fpa(5120, 1024),
                                                     fpa(7168, 1024)),
                              deps=[lnap, xld])
                    prev_ap = ap_
                    stx = dma(xpd[tg * 128:(tg + 1) * 128, :], fpa(6144, 1024),
                              deps=[ap_])
                    prev_stx = stx

                    sq2x = sop(lambda s: s.activation(
                        fpa(7168, 1024), fpa(6144, 1024), AF.Square), deps=[ap_])

                    l2 = vchain([
                        lambda v: v.tensor_reduce(rw[:, 10:11], fpa(6144, 1024),
                                                  axis=AX, op=ALU.add),
                        lambda v: v.tensor_reduce(rw[:, 11:12], fpa(7168, 1024),
                                                  axis=AX, op=ALU.add),
                        lambda v: v.tensor_scalar_mul(rw[:, 12:13], rw[:, 10:11], 1.0 / E),
                        lambda v: v.tensor_scalar_mul(rw[:, 13:14], rw[:, 11:12], 1.0 / E),
                        lambda v: v.scalar_tensor_tensor(rw[:, 14:15], rw[:, 12:13], 1.0,
                                                         rw[:, 12:13], ALU.mult, ALU.mult),
                        lambda v: v.tensor_tensor(rw[:, 15:16], rw[:, 13:14],
                                                  rw[:, 14:15], ALU.subtract),
                        lambda v: v.tensor_scalar_add(rw[:, 15:16], rw[:, 15:16], EPS),
                    ], deps=[ap_, sq2x])
                    sq2 = sop(lambda s: s.activation(rw[:, 16:17], rw[:, 15:16], AF.Sqrt),
                              deps=[l2])
                    vr2 = vop(lambda v: v.reciprocal(rw[:, 17:18], rw[:, 16:17]),
                              deps=[sq2])
                    lau2 = gop(lambda g: (g.tensor_copy(rw[:, 26:27], rw[:, 12:13]),
                                          g.tensor_copy(rw[:, 27:28], rw[:, 17:18]))[-1],
                               deps=[vr2])
                    ln2ap = vop(lambda v: v.tensor_scalar(
                        fpa(7168, 1024), fpa(6144, 1024), rw[:, 26:27], rw[:, 27:28],
                        ALU.subtract, ALU.mult), deps=[lau2, stx])
                    vr3 = vchain([
                        lambda v: v.tensor_reduce(rw[:, 18:19], fpa(7168, 1024), axis=AX,
                                                  op=ALU.max, apply_absolute_value=True),
                        lambda v: v.tensor_scalar_max(rw[:, 18:19], rw[:, 18:19], 1e-5),
                        lambda v: v.reciprocal(rw[:, 19:20], rw[:, 18:19]),
                        lambda v, tg=tg: v.scalar_tensor_tensor(
                            inv1c[:, tg:tg + 1], rw[:, 18:19], 1.0 / 127.0,
                            iswc[:, 0:1], ALU.mult, ALU.mult),
                        lambda v: v.tensor_scalar_mul(rw[:, 20:21], rw[:, 19:20], 127.0),
                    ], deps=[ln2ap])
                    lau3 = gop(lambda g, tg=tg: (
                        g.tensor_copy(rw[:, 28:29], rw[:, 20:21]),
                        g.tensor_copy(invg[:, tg:tg + 1], inv1c[:, tg:tg + 1]))[-1],
                        deps=[vr3])
                    inv_lau[tg] = lau3
                    sc1 = vop(lambda v: v.tensor_scalar(
                        fpa(7168, 1024), fpa(7168, 1024), rw[:, 28:29], None, ALU.mult),
                        deps=[lau3])

                    qn0 = vop(lambda v: v.tensor_copy(QIa[:, 0:1024], fpa(7168, 1024)),
                              deps=[sc1])
                    qn = vop(lambda v: (
                        v.tensor_tensor(fpa(10240, 1024), fpa(7168, 1024),
                                        QIa[:, 0:1024], ALU.subtract),
                        v.scalar_tensor_tensor(fpa(11264, 1024), fpa(10240, 1024), 0.5,
                                               QIa[:, 0:1024], ALU.is_gt, ALU.add),
                        v.scalar_tensor_tensor(fpa(11264, 1024), fpa(10240, 1024), -0.5,
                                               fpa(11264, 1024), ALU.is_ge, ALU.add),
                        v.tensor_scalar(fpa(8192, 1024), fpa(11264, 1024), 1.0,
                                        None, ALU.subtract))[-1], deps=[qn0])
                    prev_qn = qn
                    if DBG_SCRATCH and b == 1:
                        if "dbg_xq" not in dbg:
                            DBG("xq", [T, E])
                        prev_qn = dma(dbg["dbg_xq"][tg * 128:(tg + 1) * 128, :],
                                      fpa(8192, 1024), deps=[qn])
                    for jc in range(EC):
                        e3 = transpose_f32(lambda: fpa(9216, 128),
                                           lambda jc=jc: fpa(8192 + jc * 128, 128),
                                           deps=[qn])
                        cb_ = vop(lambda v, jc=jc, tg=tg: v.tensor_copy(
                            bfa(Q1T + jc * 2048 + tg * 128, 128), fpa(9216, 128)),
                            deps=[e3])
                        tr_state["rot"][(tr_state["cnt"] - 1) % 2] = cb_
                        q1_tail = cb_
            A.barrier()

            if debug_stage == 6:
                stop_here()
                return nc, dbg
            wld = [dma(bfa(jc * 4096, 4096), w1Td[jc]) for jc in range(EC)]
            ibc_done = []
            dslot_tail = None
            for tt in range(TT):
                d1 = vop(lambda v, tt=tt: v.tensor_scalar(
                    fpa(10240, 128), onesf[:, 0:128], invg[:, tt:tt + 1], None,
                    ALU.mult), deps=[c_ones, dslot_tail, inv_lau[tt]])
                e4 = transpose_f32(lambda tt=tt: fpa(0 + tt * 128, 128),
                                   lambda: fpa(10240, 128), deps=[d1])
                dslot_tail = e4
                ibc_done.append(e4)
            hm_dep = None
            sl_tail = None
            conv_tail = None
            ps_dep = {0: None, 1: None, 2: None, 3: None}
            for hc in range(HC):
                evs = []
                for qc in range(QB):
                    bank = qc

                    def mmh(t, hc=hc, qc=qc, bank=bank):
                        ins = None
                        for jc in range(EC):
                            lh = bfa(jc * 4096 + hc * 128, 128)
                            rh = bfa(Q1T + jc * 2048 + qc * 512, 512)
                            ins = t.matmul(PS[bank][:], lh, rh, start=(jc == 0),
                                           stop=(jc == EC - 1))
                        return ins
                    mm = top(mmh, deps=wld + [ps_dep[bank]])
                    ev = vop(lambda v, qc=qc, bank=bank: v.tensor_tensor(
                        fpa(2048 + qc * 512, 512), PS[bank][:],
                        fpa(qc * 512, 512), ALU.mult),
                        deps=[mm, conv_tail] + ibc_done)
                    ps_dep[bank] = ev
                    evs.append(ev)
                if DBG_SCRATCH and b == 1:
                    if "dbg_h" not in dbg:
                        DBG("h", [HC, 128, T])
                    dmph = dma(dbg["dbg_h"][hc], fpa(2048, 2048), deps=evs)
                    for bank in range(QB):
                        ps_dep[bank] = dmph

                cv0 = vop(lambda v, hc=hc: v.tensor_scalar(
                    fpa(4096, 2048), fpa(2048, 2048), cwcg[:, 3 * hc + 1:3 * hc + 2],
                    cbcg[:, hc:hc + 1], ALU.mult, ALU.add),
                    deps=evs + [g_cw, g_cb, sl_tail])
                cv1 = vop(lambda v, hc=hc: v.scalar_tensor_tensor(
                    FPA[:, 4097:6144], FPA[:, 2048:4095], cwcg[:, 3 * hc:3 * hc + 1],
                    FPA[:, 4097:6144], ALU.mult, ALU.add), deps=[cv0])
                cv = vop(lambda v, hc=hc: v.scalar_tensor_tensor(
                    FPA[:, 4096:6143], FPA[:, 2049:4096], cwcg[:, 3 * hc + 2:3 * hc + 3],
                    FPA[:, 4096:6143], ALU.mult, ALU.add), deps=[cv1])
                conv_tail = cv
                dmhc = None
                if DBG_SCRATCH and b == 1:
                    if "dbg_hc" not in dbg:
                        DBG("hc", [HC, 128, T])
                    dmhc = dma(dbg["dbg_hc"][hc], fpa(4096, 2048), deps=[cv])
                sl_ = sop(lambda s: s.activation(fpa(6144, 2048), fpa(4096, 2048),
                                                 AF.Silu), deps=[cv, sl_tail])
                def maxfold(v, hc=hc):
                    if hc == 0:
                        return v.scalar_tensor_tensor(fpa(8192, 2048), fpa(6144, 2048),
                                                      -1.0, fpa(6144, 2048),
                                                      ALU.mult, ALU.max)
                    v.scalar_tensor_tensor(fpa(4096, 2048), fpa(6144, 2048), -1.0,
                                           fpa(6144, 2048), ALU.mult, ALU.max)
                    return v.tensor_tensor(fpa(8192, 2048), fpa(8192, 2048),
                                           fpa(4096, 2048), ALU.max)
                mf = vop(maxfold, deps=[sl_, hm_dep, dmhc])
                hm_dep = mf
                sst = dma(sld[hc], fpa(6144, 2048), deps=[mf])
                sl_tail = sst
            c2bc_done = []
            dslot_tail = None
            ct_lau = [None] * TT
            for tt in range(TT):
                e5 = transpose_f32(lambda: fpa(10240, 128),
                                   lambda tt=tt: fpa(8192 + tt * 128, 128),
                                   deps=[hm_dep, dslot_tail])
                r5 = vchain([
                    lambda v: v.tensor_reduce(rw[:, 21:22], fpa(10240, 128), axis=AX,
                                              op=ALU.max),
                    lambda v: v.tensor_scalar_max(rw[:, 21:22], rw[:, 21:22], 1e-5),
                    lambda v: v.reciprocal(rw[:, 22:23], rw[:, 21:22]),
                    lambda v, tt=tt: v.scalar_tensor_tensor(
                        ctc[:, tt:tt + 1], rw[:, 21:22], 1.0 / 127.0, iswc[:, 1:2],
                        ALU.mult, ALU.mult),
                    lambda v, tt=tt: v.tensor_scalar_mul(c2c[:, tt:tt + 1],
                                                         rw[:, 22:23], 127.0),
                ], deps=[e5])
                tr_state["rot"][(tr_state["cnt"] - 1) % 2] = r5
                lau4 = gop(lambda g, tt=tt: (
                    g.tensor_copy(rw[:, 29:30], c2c[:, tt:tt + 1]),
                    g.tensor_copy(ctcg[:, tt:tt + 1], ctc[:, tt:tt + 1]))[-1],
                    deps=[r5])
                ct_lau[tt] = lau4
                d2 = vop(lambda v: v.tensor_scalar(
                    fpa(10240, 128), onesf[:, 0:128], rw[:, 29:30], None, ALU.mult),
                    deps=[lau4])
                e6 = transpose_f32(lambda tt=tt: fpa(11264 + tt * 128, 128),
                                   lambda: fpa(10240, 128), deps=[d2])
                dslot_tail = e6
                c2bc_done.append(e6)
            A.barrier()

            if debug_stage == 7:
                stop_here()
                return nc, dbg
            wld = [dma(bfa(hc2 * 1024, 1024), w2Td[hc2]) for hc2 in range(HC)]
            prev_tail = None
            prev_last_af = None
            for qb in range(QB):
                q2_tail = None
                mm_last = None
                for hc in range(HC):
                    sl = hc % 2
                    sdl = dma(fpa(sl * 512, 512), sld[hc][:, qb * 512:qb * 512 + 512],
                              deps=[q2_tail])

                    qa = vop(lambda v, sl=sl, qb=qb: v.tensor_tensor(
                        fpa(1024, 512), fpa(sl * 512, 512),
                        fpa(11264 + qb * 512, 512), ALU.mult),
                        deps=([sdl, mm_last] + c2bc_done +
                              (prev_tail if (hc == 0 and prev_tail) else [])))
                    qb_ = vop(lambda v: v.tensor_copy(QIa[:, 0:512], fpa(1024, 512)),
                              deps=[qa])
                    qc2 = vop(lambda v: (
                        v.tensor_tensor(fpa(1536, 512), fpa(1024, 512),
                                        QIa[:, 0:512], ALU.subtract),
                        v.scalar_tensor_tensor(fpa(2048, 512), fpa(1536, 512), 0.5,
                                               QIa[:, 0:512], ALU.is_gt, ALU.add),
                        v.scalar_tensor_tensor(fpa(2048, 512), fpa(1536, 512), -0.5,
                                               fpa(2048, 512), ALU.is_ge, ALU.add),
                        v.tensor_scalar(fpa(2560, 512), fpa(2048, 512), 1.0,
                                        None, ALU.subtract))[-1], deps=[qb_])
                    q2_ = vop(lambda v: v.tensor_copy(bfa(32768, 512), fpa(2560, 512)),
                              deps=[qc2])

                    def mmo(t, hc=hc, first=(hc == 0), last=(hc == HC - 1)):
                        ins = None
                        for ic in range(EC):
                            lh = bfa(hc * 1024 + ic * 128, 128)
                            ins = t.matmul(PS[ic][:], lh, bfa(32768, 512),
                                           start=first, stop=last)
                        return ins
                    deps = wld + [q2_]
                    if hc == 0 and prev_tail is not None:
                        deps = deps + prev_tail
                    mm_last = top(mmo, deps=deps)
                    q2_tail = mm_last
                o2e = [vop(lambda v, ic=ic: v.tensor_copy(
                    fpa(1536 + ic * 512, 512), PS[ic][:]), deps=[mm_last])
                    for ic in range(EC)]
                atm2 = []
                for ic in range(EC):
                    for ts_ in range(4):
                        e7 = transpose_f32(
                            lambda ic=ic, ts_=ts_: FPA[:, 5632 + ts_ * 1024 + ic * 128:
                                                       5632 + ts_ * 1024 + ic * 128 + 128],
                            lambda ic=ic, ts_=ts_: fpa(1536 + ic * 512 + ts_ * 128, 128),
                            deps=[o2e[ic], o2e[6], o2e[7]])
                        atm2.append(e7)
                af_list = []
                sto_list = []
                for ts_ in range(4):
                    tg = qb * 4 + ts_
                    xpl = dma(fpa(9728, 1024), xpd[tg * 128:(tg + 1) * 128, :],
                              deps=[af_list[-1] if af_list else prev_last_af])

                    zs_ = vop(lambda v, ts_=ts_, tg=tg: v.tensor_scalar(
                        FPA[:, 5632 + ts_ * 1024:6656 + ts_ * 1024],
                        FPA[:, 5632 + ts_ * 1024:6656 + ts_ * 1024],
                        ctcg[:, tg:tg + 1], None, ALU.mult),
                        deps=atm2 + [ct_lau[tg]])
                    sqF = sop(lambda s, ts_=ts_: s.activation(
                        fpa(1536, 1024), FPA[:, 5632 + ts_ * 1024:6656 + ts_ * 1024],
                        AF.Square), deps=[zs_])

                    lf = vchain([
                        lambda v, ts_=ts_: v.tensor_reduce(
                            rw[:, 0:1], FPA[:, 5632 + ts_ * 1024:6656 + ts_ * 1024],
                            axis=AX, op=ALU.add),
                        lambda v: v.tensor_reduce(rw[:, 1:2], fpa(1536, 1024), axis=AX,
                                                  op=ALU.add),
                        lambda v: v.tensor_scalar_mul(rw[:, 2:3], rw[:, 0:1], 1.0 / E),
                        lambda v: v.tensor_scalar_mul(rw[:, 3:4], rw[:, 1:2], 1.0 / E),
                        lambda v: v.scalar_tensor_tensor(rw[:, 4:5], rw[:, 2:3], 1.0,
                                                         rw[:, 2:3], ALU.mult, ALU.mult),
                        lambda v: v.tensor_tensor(rw[:, 5:6], rw[:, 3:4], rw[:, 4:5],
                                                  ALU.subtract),
                        lambda v: v.tensor_scalar_add(rw[:, 5:6], rw[:, 5:6], EPS),
                    ], deps=[zs_, sqF])
                    sqf = sop(lambda s: s.activation(rw[:, 6:7], rw[:, 5:6], AF.Sqrt),
                              deps=[lf])
                    vrf = vop(lambda v: v.reciprocal(rw[:, 7:8], rw[:, 6:7]),
                              deps=[sqf])
                    lauf = gop(lambda g: (g.tensor_copy(rw[:, 24:25], rw[:, 2:3]),
                                          g.tensor_copy(rw[:, 25:26], rw[:, 7:8]))[-1],
                               deps=[vrf])
                    lfap = vop(lambda v, ts_=ts_: v.tensor_scalar(
                        FPA[:, 5632 + ts_ * 1024:6656 + ts_ * 1024],
                        FPA[:, 5632 + ts_ * 1024:6656 + ts_ * 1024],
                        rw[:, 24:25], rw[:, 25:26], ALU.subtract, ALU.mult),
                        deps=[lauf])
                    af = vop(lambda v, ts_=ts_: v.tensor_add(
                        FPA[:, 5632 + ts_ * 1024:6656 + ts_ * 1024], fpa(9728, 1024),
                        FPA[:, 5632 + ts_ * 1024:6656 + ts_ * 1024]),
                        deps=[lfap, xpl])
                    af_list.append(af)
                    sto = dma(y_out[b, tg * 128:(tg + 1) * 128, :],
                              FPA[:, 5632 + ts_ * 1024:6656 + ts_ * 1024], deps=[af])
                    last_out = sto
                    sto_list.append(sto)
                prev_tail = af_list + atm2 + sto_list
                prev_last_af = af_list[-1]
            A.barrier()

        gop(lambda g: g.nop(), deps=[last_out])
        A.emit(nc, block, es)
    return nc, dbg


_CACHE = {}


def _get_nc():
    if "nc" not in _CACHE:
        _CACHE["nc"] = build(0)[0]
    return _CACHE["nc"]


def kernel(L0_attn_x, L0_attn_Wk, L0_attn_Wq, L0_attn_Wv, L0_attn_Wu, L0_attn_bu,
           L0_ffn_w1, L0_ffn_conv_w, L0_ffn_conv_b, L0_ffn_w2):
    x = np.ascontiguousarray(np.asarray(L0_attn_x, dtype=np.float32))
    shards = [np.ascontiguousarray(x[2 * c:2 * c + 2]) for c in range(NCORES)]
    com = dict(
        Wk=np.ascontiguousarray(np.asarray(L0_attn_Wk, np.float32)),
        Wq=np.ascontiguousarray(np.asarray(L0_attn_Wq, np.float32)),
        Wv=np.ascontiguousarray(np.asarray(L0_attn_Wv, np.float32)),
        Wu=np.ascontiguousarray(np.asarray(L0_attn_Wu, np.float32)),
        bu=np.ascontiguousarray(np.asarray(L0_attn_bu, np.float32)),
        w1=np.ascontiguousarray(np.asarray(L0_ffn_w1, np.float32)),
        cw=np.ascontiguousarray(np.asarray(L0_ffn_conv_w, np.float32).reshape(DH, 3)),
        cb=np.ascontiguousarray(np.asarray(L0_ffn_conv_b, np.float32)),
        w2=np.ascontiguousarray(np.asarray(L0_ffn_w2, np.float32)),
    )
    nc = _get_nc()
    in_maps = [dict(x=shards[c], **com) for c in range(NCORES)]
    res = run_bass_kernel_spmd(nc, in_maps, core_ids=list(range(NCORES)))
    return np.concatenate([np.asarray(r["y"]) for r in res.results], axis=0)
